# version 26
# baseline (speedup 1.0000x reference)
import numpy as np
import concourse.bass as bass
import concourse.mybir as mybir
from concourse.bass_utils import run_bass_kernel_spmd
from contextlib import ExitStack

F32 = mybir.dt.float32
BF16 = mybir.dt.bfloat16
AX = mybir.AxisListType
ALU = mybir.AluOpType
ACT = mybir.ActivationFunctionType

NH = 8
DK = 128
CH = 64
SEG = 2048
NT = 4
TT = 512
NCH = 8
D = 1024
E = 8
HID = 2048
QSC = float(DK) ** -0.5

DBG_ON = False

ENGINES = ["tensor", "vector", "scalar", "gpsimd", "sync", "cc"]
RUN_ENGINES = ["tensor", "vector", "scalar", "gpsimd", "sync"]


class Prog:
    def __init__(self):
        self.q = {e: [] for e in RUN_ENGINES}
        self.cnt = {e: 0 for e in ENGINES}
        self.wm = {e: {p: 0 for p in ENGINES} for e in ENGINES}

    def emit(self, eng, fn, dma=False, deps=(), inc_eng=None, inc=None):
        if inc is None:
            inc = 16 if dma else 1
        if inc_eng is None:
            inc_eng = eng
        self.cnt[inc_eng] += inc
        target = self.cnt[inc_eng]
        best = {}
        for (de, dv) in deps:
            if (de == eng and de not in ("gpsimd", "sync")) or dv <= 0:
                continue
            if dv > best.get(de, 0):
                best[de] = dv
        wm = self.wm[eng]
        final = [(de, dv) for de, dv in best.items() if dv > wm[de]]
        for de, dv in final:
            wm[de] = dv

        def thunk(engine, sems, final=final, fn=fn, inc_eng=inc_eng, inc=inc):
            for (de, dv) in final:
                engine.wait_ge(sems[de], dv)
            fn(engine).then_inc(sems[inc_eng], inc)

        self.q[eng].append(thunk)
        return (inc_eng, target)

    def dep(self, eng):
        return (eng, self.cnt[eng])

    def barrier(self, exclude=()):
        return [self.dep(e) for e in ENGINES if e not in exclude]


class Res:
    __slots__ = ("w", "r")

    def __init__(self):
        self.w = []
        self.r = []

    def wdeps(self):
        return self.r + self.w

    def rdeps(self):
        return list(self.w)

    def wrote(self, d):
        self.w = [d]
        self.r = []

    def read(self, d):
        self.r.append(d)


def build():
    nc = bass.Bass("TRN2", target_bir_lowering=False)
    P = Prog()

    def _ew(eng, fn, w=(), r=(), deps=(), dma=False, inc_eng=None, inc=None):
        dd = list(deps)
        for res in w:
            dd += res.wdeps()
        for res in r:
            dd += res.rdeps()
        d = P.emit(eng, fn, dma=dma, deps=dd, inc_eng=inc_eng, inc=inc)
        for res in w:
            res.wrote(d)
        for res in r:
            res.read(d)
        return d

    xT = nc.dram_tensor("xT", [D, SEG], F32, kind="ExternalInput").ap()
    wq_d = nc.dram_tensor("wq", [D, D], F32, kind="ExternalInput").ap()
    wk_d = nc.dram_tensor("wk", [D, D], F32, kind="ExternalInput").ap()
    wv_d = nc.dram_tensor("wv", [D, D], F32, kind="ExternalInput").ap()
    wo_d = nc.dram_tensor("wo", [D, D], F32, kind="ExternalInput").ap()
    w1t_d = nc.dram_tensor("w1t", [E, D, HID], F32, kind="ExternalInput").ap()
    w2t_d = nc.dram_tensor("w2t", [E, HID, D], F32, kind="ExternalInput").ap()
    rwt_d = nc.dram_tensor("rwt", [D, E], F32, kind="ExternalInput").ap()
    eye_d = nc.dram_tensor("eye", [128, 128], F32, kind="ExternalInput").ap()
    cdt_d = nc.dram_tensor("cdt", [128, 512], F32, kind="ExternalInput").ap()
    cdq_d = nc.dram_tensor("cdq", [128, 512], F32, kind="ExternalInput").ap()
    cdq8_d = nc.dram_tensor("cdq8", [128, 8, 512], F32, kind="ExternalInput").ap()
    cdk_d = nc.dram_tensor("cdk", [128, 8], F32, kind="ExternalInput").ap()
    ccoef_d = nc.dram_tensor("ccoef", [128, 32], F32, kind="ExternalInput").ap()
    outT = nc.dram_tensor("outT", [D, SEG], F32, kind="ExternalOutput").ap()
    dbg_t = {}

    def dbg(name, shape, dt, src_ap, rres, deps=()):
        if not DBG_ON:
            return
        if name not in dbg_t:
            dbg_t[name] = nc.dram_tensor("dbg_" + name, shape, dt, kind="ExternalOutput").ap()
        _ew("sync", lambda s, a=dbg_t[name], b=src_ap: s.dma_start(out=a, in_=b),
            r=list(rres), deps=deps, dma=True)

    cc_in = nc.dram_tensor("cc_in", [NH * DK, DK], F32)
    cc_out = nc.dram_tensor("cc_out", [4 * NH * DK, DK], F32)

    ds = [float(1.0 - 2.0 ** (-5.0 - h)) ** CH for h in range(NH)]

    es0 = ExitStack()
    esA1 = ExitStack()
    esA2 = ExitStack()
    esB = ExitStack()

    def sb(es, name, shape, dt, side="left"):
        return es.enter_context(nc.sbuf_tensor(name, shape, dt, side=side))

    cdt_sb = sb(es0, "cdt_sb", [128, 512], BF16)
    cdq_sb = sb(es0, "cdq_sb", [128, 512], BF16)
    cdk_sb = sb(es0, "cdk_sb", [128, 8], F32)
    ccoef_sb = sb(es0, "ccoef_sb", [128, 32], F32)
    eyeb = sb(es0, "eyeb", [128, 128], BF16)
    rwt_sb = sb(es0, "rwt_sb", [128, 8, 8], BF16)
    ones_b = sb(es0, "ones_b", [128, 1], BF16)
    ones1b = sb(es0, "ones1b", [1, 128], BF16)
    ones_f = sb(es0, "ones_f", [1, 128], F32)
    mrow = sb(es0, "mrow", [1, 1024], F32)
    mrow2 = sb(es0, "mrow2", [1, 1024], F32)
    tmprow = sb(es0, "tmprow", [1, 1024], F32)
    gll = sb(es0, "gll", [128, 16, 8], F32)
    gm1 = sb(es0, "gm1", [128, 16, 1], F32)
    geq = sb(es0, "geq", [128, 16, 8], F32)
    gl3 = sb(es0, "gl3", [128, 16, 8], F32)
    gm2 = sb(es0, "gm2", [128, 16, 1], F32)
    gtl = sb(es0, "gtl", [128, 16, 8], F32)
    gp = sb(es0, "gp", [128, 16, 8], F32)
    gmask = sb(es0, "gmask", [128, 16, 8], F32)
    gpm = sb(es0, "gpm", [128, 16, 8], F32)
    gZ = sb(es0, "gZ", [128, 16, 1], F32)
    grZ = sb(es0, "grZ", [128, 16, 1], F32)
    gatb = sb(es0, "gatb", [128, 16, 8], BF16)
    g1row = sb(es0, "g1row", [1, 512], F32)

    xbt = [sb(esA2, f"xbt{i}", [128, 8, 512], BF16) for i in range(2)]
    otc = sb(esA2, "otc", [128, 8, SEG], BF16)
    x1 = sb(esA2, "x1", [128, 8, 512], BF16)
    qt2 = [sb(esA2, f"qt{i}", [128, 8, 512], BF16) for i in range(2)]
    qt = qt2[0]
    qd2 = [sb(esA2, f"qd{i}", [128, 512], BF16) for i in range(2)]
    wq_sb = sb(esA2, "wq_sb", [128, 8, 1024], BF16)
    mB = sb(esA2, "mB", [128, 512], BF16)
    rB = sb(esA2, "rB", [128, 512], BF16)
    m2B = sb(esA2, "m2B", [128, 512], BF16)
    r2B = sb(esA2, "r2B", [128, 512], BF16)
    sqb = sb(esA2, "sqb", [128, 512], BF16)
    Sf = sb(esA1, "Sf", [128, NH * DK], F32, side="right")
    Sb = sb(esA1, "Sb", [128, NH * DK], BF16, side="right")
    wk_sb = sb(esA1, "wk_sb", [128, 8, 1024], BF16, side="right")
    wv_sb = sb(esA1, "wv_sb", [128, 8, 1024], BF16, side="right")
    kfm = sb(esA1, "kfm", [128, 8, 512], BF16, side="right")
    kd = sb(esA1, "kd", [128, 4, 1024], BF16, side="right")
    vtk = sb(esA1, "vtk", [128, 4, 1024], BF16, side="right")
    sT = sb(esA1, "sT", [128, 512], BF16, side="right")

    pb = [es0.enter_context(nc.psum_tensor(f"pb{i}", [128, 512], F32)) for i in range(8)]
    rpb = [Res() for _ in range(8)]

    sems_h = {}
    for e in ENGINES:
        sems_h[e] = es0.enter_context(nc.semaphore(f"sem_{e}"))

    r_xbt = [Res(), Res()]
    r_otc = Res()
    r_x1 = Res()
    r_qt2 = [Res(), Res()]
    r_qt = r_qt2[0]
    r_qd2 = [Res(), Res()]
    r_kfm = Res()
    r_kd = Res()
    r_vtk = Res()
    r_sT = Res()
    r_Sf = [Res() for _ in range(NH)]
    r_Sb = [Res() for _ in range(NH)]
    r_SEf = Res()
    r_SEb = Res()
    r_mB = Res()
    r_rB = Res()
    r_m2B = Res()
    r_r2B = Res()
    r_sqb = Res()
    r_mrow = Res()
    r_mrow2 = Res()
    r_tmprow = Res()
    r_x2n = Res()
    r_gch = Res()
    r_g1 = Res()
    r_w = {n: Res() for n in ["wq", "wk", "wv", "wo", "rwt", "consts"]}
    r_out = {}

    def mm(lhsT, rhs, bank, cols, start, stop, deps=(), rres=(), tile_position=None):
        if isinstance(cols, tuple):
            out = pb[bank][cols[0], cols[1]]
        else:
            out = pb[bank][:, cols]
        dd = list(deps)
        if start:
            dd += rpb[bank].wdeps()
        for res in rres:
            dd += res.rdeps()
        d = P.emit("tensor", lambda t, o=out, l=lhsT, r=rhs, s1=start, s2=stop, tp=tile_position:
                   t.matmul(o, l, r, start=s1, stop=s2, tile_position=tp), deps=dd)
        rpb[bank].wrote(d)
        for res in rres:
            res.read(d)
        return d

    def g_dma(dst, src, res, deps=()):
        return _ew("gpsimd", lambda g, a=dst, b=src: g.dma_start(out=a, in_=b),
                   w=[res], deps=deps, dma=True)

    def s_dma(dst, src, wres=(), rres=(), deps=()):
        return _ew("sync", lambda s, a=dst, b=src: s.dma_start(out=a, in_=b),
                   w=wres, r=rres, deps=deps, dma=True)

    wq_r = wq_d.rearrange("(k p) o -> p k o", p=128)
    wk_r = wk_d.rearrange("(k p) o -> p k o", p=128)
    wv_r = wv_d.rearrange("(k p) o -> p k o", p=128)
    wo_r = wo_d.rearrange("(k p) o -> p k o", p=128)
    xT_r = xT.rearrange("(k p) t -> p k t", p=128)

    _ew("vector", lambda v: v.memset(ones_b[:], 1.0), w=[r_w["consts"]])
    _ew("vector", lambda v: v.memset(ones1b[:], 1.0), w=[r_w["consts"]])
    _ew("vector", lambda v: v.memset(ones_f[:], 1.0), w=[r_w["consts"]])
    _ew("vector", lambda v: v.memset(Sf[:], 0.0), w=r_Sf)
    _ew("vector", lambda v: v.memset(Sb[:], 0.0), w=r_Sb)
    g_dma(eyeb[:], eye_d, r_w["consts"])
    g_dma(cdt_sb[:], cdt_d, r_w["consts"])
    g_dma(cdq_sb[:], cdq_d, r_w["consts"])
    s_dma(cdk_sb[:], cdk_d, wres=[r_w["consts"]])
    s_dma(ccoef_sb[:], ccoef_d, wres=[r_w["consts"]])
    g_dma(rwt_sb[:], rwt_d.rearrange("(k p) e -> p k e", p=128), r_w["rwt"])
    g_dma(wk_sb[:], wk_r, r_w["wk"])
    g_dma(wv_sb[:], wv_r, r_w["wv"])
    g_dma(wq_sb[:], wq_r, r_w["wq"])

    def ln_stats(t, src3, res_src, which, extra=()):
        mrow_t = mrow if which == 1 else mrow2
        r_mr = r_mrow if which == 1 else r_mrow2
        for kc in range(8):
            mm(ones_b[:], src3(kc), 0, (slice(0, 1), slice(0, 512)), kc == 0, kc == 7,
               rres=[res_src, r_w["consts"]], deps=list(extra) if kc == 0 else ())
            _ew("vector", lambda v, kc=kc: v.tensor_mul(sqb[:], src3(kc), src3(kc)),
                w=[r_sqb], r=[res_src])
            mm(ones_b[:], sqb[:], 1, (slice(0, 1), slice(0, 512)), kc == 0, kc == 7,
               rres=[r_sqb, r_w["consts"]])
        base = 0
        m_ap = mrow_t[0:1, base:base + 512]
        r_ap = mrow_t[0:1, base + 512:base + 1024]
        _ew("vector", lambda v: v.tensor_scalar_mul(m_ap, pb[0][0:1, 0:512], 1.0 / D),
            w=[r_mr], r=[rpb[0]])
        _ew("vector", lambda v: v.tensor_scalar_mul(tmprow[0:1, 0:512], pb[1][0:1, 0:512], 1.0 / D),
            w=[r_tmprow], r=[rpb[1]])
        _ew("vector", lambda v: v.drain())
        _ew("vector", lambda v: v.tensor_mul(tmprow[0:1, 512:1024], m_ap, m_ap),
            w=[r_tmprow], r=[r_mr])
        _ew("vector", lambda v: v.drain())
        _ew("vector", lambda v: v.scalar_tensor_tensor(
            g1row[0:1, 0:512], tmprow[0:1, 512:1024], -1.0, tmprow[0:1, 0:512],
            ALU.mult, ALU.add), w=[r_g1], r=[r_tmprow])
        _ew("vector", lambda v: v.drain())
        _ew("vector", lambda v: v.tensor_scalar_add(tmprow[0:1, 0:512], g1row[0:1, 0:512],
                                                    1e-5), w=[r_tmprow], r=[r_g1])
        _ew("vector", lambda v: v.drain())
        _ew("scalar", lambda s: s.activation(out=tmprow[0:1, 512:1024], in_=tmprow[0:1, 0:512],
                                             func=ACT.Ln, bias=0.0, scale=1.0),
            w=[r_tmprow], r=[r_tmprow])
        _ew("scalar", lambda s: s.activation(out=r_ap, in_=tmprow[0:1, 512:1024],
                                             func=ACT.Exp, bias=0.0, scale=-0.5),
            w=[r_mr], r=[r_tmprow])

    def ln_bcast(t, which, bm=None, br=None, rm=None, rr=None):
        mrow_t = mrow if which == 1 else mrow2
        r_mr = r_mrow if which == 1 else r_mrow2
        bm = bm if bm is not None else mB
        br = br if br is not None else rB
        rm = rm if rm is not None else r_mB
        rr = rr if rr is not None else r_rB
        mm(ones_f[:], mrow_t[0:1, 0:512], 0, slice(0, 512), True, True,
           rres=[r_mr, r_w["consts"]])
        _ew("vector", lambda v: v.tensor_copy(bm[:], pb[0][:, 0:512]), w=[rm], r=[rpb[0]])
        mm(ones_f[:], mrow_t[0:1, 512:1024], 1, slice(0, 512), True, True,
           rres=[r_mr, r_w["consts"]])
        _ew("vector", lambda v: v.tensor_copy(br[:], pb[1][:, 0:512]), w=[rr], r=[rpb[1]])

    def ln_apply(dst3, res_dst, src3, res_src, bm=None, br=None, rm=None, rr=None):
        bm = bm if bm is not None else mB
        br = br if br is not None else rB
        rm = rm if rm is not None else r_mB
        rr = rr if rr is not None else r_rB
        for kc in range(8):
            _ew("vector", lambda v, kc=kc: v.tensor_tensor(sqb[:], src3(kc), bm[:], ALU.subtract),
                w=[r_sqb], r=[res_src, rm])
            _ew("vector", lambda v, kc=kc: v.tensor_tensor(dst3(kc), sqb[:], br[:], ALU.mult),
                w=[res_dst], r=[r_sqb, rr])

    def proj_q_grp(jc, dst, rdst):
        bank = jc % 2
        for kc in range(8):
            mm(wq_sb[:, kc, jc * 128:(jc + 1) * 128], x1[:, kc, :], bank, slice(0, 512),
               kc == 0, kc == 7, rres=[r_w["wq"], r_x1])
        _ew("scalar", lambda s, jc=jc, bank=bank, dst=dst: s.activation(
            out=dst[:, jc, :], in_=pb[bank][:, 0:512], func=ACT.Copy, bias=0.0, scale=QSC),
            w=[rdst], r=[rpb[bank]])

    def proj_q(dst=None, rdst=None):
        dst = dst if dst is not None else qt
        rdst = rdst if rdst is not None else r_qt
        for jc in range(8):
            proj_q_grp(jc, dst, rdst)

    def load_x(t):
        g_dma(xbt[t % 2][:], xT_r[:, :, t * TT:(t + 1) * TT], r_xbt[t % 2])

    load_x(0)
    load_x(1)
    for t in range(NT):
        sl = r_xbt[t % 2]

        def xsrc(kc, t=t):
            return xbt[t % 2][:, kc, :]

        tb = [P.dep("tensor"), P.dep("vector"), P.dep("scalar")] if t > 0 else []
        ln_stats(t, xsrc, sl, 1, extra=tb)
        ln_bcast(t, 1)
        ln_apply(lambda kc: x1[:, kc, :], r_x1, xsrc, sl)
        if t + 2 < NT:
            load_x(t + 2)
        proj_q(qt2[t % 2], r_qt2[t % 2])
        for jc in range(8):
            bank = jc % 2
            for kc in range(8):
                mm(wk_sb[:, kc, jc * 128:(jc + 1) * 128], x1[:, kc, :], bank, slice(0, 512),
                   kc == 0, kc == 7, rres=[r_w["wk"], r_x1])
            _ew("scalar", lambda s, jc=jc, bank=bank: s.copy(out=kfm[:, jc, :],
                                                             in_=pb[bank][:, 0:512]),
                w=[r_kfm], r=[rpb[bank]])
        for c4 in range(4):
            for half in range(2):
                bank = (c4 * 2 + half) % 2
                for kc in range(8):
                    mm(x1[:, kc, c4 * 128:(c4 + 1) * 128],
                       wk_sb[:, kc, half * 512:(half + 1) * 512],
                       bank, slice(0, 512), kc == 0, kc == 7, rres=[r_x1, r_w["wk"]])
                for h4 in range(4):
                    h = half * 4 + h4
                    _ew("vector", lambda v, c4=c4, h=h, h4=h4, bank=bank: v.tensor_scalar(
                        kd[:, c4, h * 128:(h + 1) * 128], pb[bank][:, h4 * 128:(h4 + 1) * 128],
                        cdk_sb[:, h:h + 1], None, ALU.mult),
                        w=[r_kd], r=[rpb[bank], r_w["consts"]])
                bankv = 1 - bank
                for kc in range(8):
                    mm(x1[:, kc, c4 * 128:(c4 + 1) * 128],
                       wv_sb[:, kc, half * 512:(half + 1) * 512],
                       bankv, slice(0, 512), kc == 0, kc == 7, rres=[r_x1, r_w["wv"]])
                _ew("vector", lambda v, c4=c4, half=half, bankv=bankv: v.tensor_copy(
                    vtk[:, c4, half * 512:(half + 1) * 512], pb[bankv][:, 0:512]),
                    w=[r_vtk], r=[rpb[bankv]])
        for n in range(NCH):
            c4, sub = n // 2, n % 2
            pa = 2 + (n % 2)
            po = 4 + (n % 2)
            _ew("vector", lambda v, n=n, qx=qt2[t % 2]: v.tensor_tensor(
                qd2[n % 2][:].rearrange("p (h i) -> p h i", i=64),
                qx[:, :, n * 64:(n + 1) * 64],
                cdq_sb[:].rearrange("p (h i) -> p h i", i=64), ALU.mult),
                w=[r_qd2[n % 2]], r=[r_qt2[t % 2], r_w["consts"]])
            pl = slice(sub * 64, (sub + 1) * 64)
            for h in range(NH):
                mm(kfm[:, h, n * 64:(n + 1) * 64], qt2[t % 2][:, h, n * 64:(n + 1) * 64],
                   pa, (pl, slice(h * 64, (h + 1) * 64)), True, True,
                   rres=[r_kfm, r_qt2[t % 2]], tile_position=(0, sub * 64))
            _ew("vector", lambda v, pa=pa, pl=pl: v.tensor_tensor(sT[pl, :], pb[pa][pl, 0:512],
                                                                  cdt_sb[pl, :], ALU.mult),
                w=[r_sT], r=[rpb[pa], r_w["consts"]])
            for h in range(NH):
                mm(vtk[pl, c4, h * 128:(h + 1) * 128],
                   sT[pl, h * 64:(h + 1) * 64],
                   po, (slice(0, 128), slice(h * 64, (h + 1) * 64)), True, False,
                   rres=[r_vtk, r_sT])
                mm(Sb[:, h * 128:(h + 1) * 128], qd2[n % 2][:, h * 64:(h + 1) * 64],
                   po, (slice(0, 128), slice(h * 64, (h + 1) * 64)), False, True,
                   rres=[r_Sb[h], r_qd2[n % 2]])
            for h in range(NH):
                _ew("scalar", lambda s, h=h, t=t, n=n, po=po: s.copy(
                    out=otc[:, h, t * TT + n * 64:t * TT + (n + 1) * 64],
                    in_=pb[po][:, h * 64:(h + 1) * 64]),
                    w=[r_otc], r=[rpb[po]])
            for h in range(NH):
                pA = 6 + (h % 2)
                mm(kd[sub * 64:(sub + 1) * 64, c4, h * 128:(h + 1) * 128],
                   vtk[sub * 64:(sub + 1) * 64, c4, h * 128:(h + 1) * 128],
                   pA, (slice(0, 128), slice((h // 2) * 128, (h // 2 + 1) * 128)), True, True,
                   rres=[r_kd, r_vtk])
                _ew("vector", lambda v, h=h, pA=pA: v.scalar_tensor_tensor(
                    Sf[:, h * 128:(h + 1) * 128], Sf[:, h * 128:(h + 1) * 128], ds[h],
                    pb[pA][:, (h // 2) * 128:(h // 2 + 1) * 128], ALU.mult, ALU.add),
                    w=[r_Sf[h]], r=[rpb[pA]])
                _ew("scalar", lambda s, h=h: s.copy(out=Sb[:, h * 128:(h + 1) * 128],
                                                    in_=Sf[:, h * 128:(h + 1) * 128]),
                    w=[r_Sb[h]], r=[r_Sf[h]])
        if t == 0:
            dbg("x1_t0", [128, 8, 512], BF16, x1[:], [r_x1])
            dbg("qt_t0", [128, 8, 512], BF16, qt[:], [r_qt])
            dbg("kfm_t0", [128, 8, 512], BF16, kfm[:], [r_kfm])
            dbg("kd_t0", [128, 4, 1024], BF16, kd[:], [r_kd])
            dbg("vtk_t0", [128, 4, 1024], BF16, vtk[:], [r_vtk])
            dbg("mrow_t0", [1, 1024], F32, mrow[:], [r_mrow])

    dbg("Sf_A", [128, NH * DK], F32, Sf[:], r_Sf)

    cc_in_r = cc_in.ap().rearrange("(h p) d -> p h d", p=128)
    d_ccin = s_dma(cc_in_r, Sf[:].rearrange("p (h d) -> p h d", d=DK), rres=r_Sf)
    d_cc = P.emit("gpsimd", lambda g: g.collective_compute(
        "AllGather", ALU.bypass, replica_groups=[[0, 1, 2, 3], [4, 5, 6, 7]],
        ins=[cc_in.ap()], outs=[cc_out.ap()]), deps=[d_ccin], inc_eng="cc", inc=1)

    barA = P.barrier(exclude=('cc',))
    esA1.close()
    x2n = sb(es0, "x2n", [128, 8, SEG], BF16, side="right")
    Tg = sb(esB, "Tg", [128, 32 * DK], BF16)
    wo_sb = sb(esB, "wo_sb", [128, 8, 1024], BF16)
    Rf = [sb(esB, f"Rf{i}", [128, 512], F32) for i in range(2)]
    Rb = [sb(esB, "Rb0", [128, 8, 512], BF16)]
    cdq8_sb = sb(esB, "cdq8_sb", [128, 8, 512], BF16)
    SEbt = sb(esB, "SEbt", [128, NH * DK], BF16)
    xf = [sb(esB, f"xf{i}", [128, 512], F32) for i in range(3)]
    SEf = sb(esB, "SEf", [128, NH * DK], F32)
    SEb = sb(esB, "SEb", [128, NH * DK], BF16)
    r_Tg = Res()
    r_Rf = [Res(), Res()]
    r_Rb = [Res()]
    r_SEbt = Res()
    r_xf = [Res() for _ in range(3)]

    g_dma(wo_sb[:], wo_r, r_w["wo"], deps=barA)
    g_dma(cdq8_sb[:], cdq8_d, r_w["consts"], deps=barA)
    def emit_SE():
        cc_out_r = cc_out.ap().rearrange("(j p) d -> p j d", p=128)
        g_dma(Tg[:].rearrange("p (j d) -> p j d", d=DK), cc_out_r, r_Tg,
              deps=barA + [d_cc])
        for h in range(NH):
            _ew("vector", lambda v, h=h: v.tensor_scalar(
                SEf[:, h * 128:(h + 1) * 128], Tg[:, h * 128:(h + 1) * 128],
                ccoef_sb[:, h:h + 1], None, ALU.mult),
                w=[r_SEf], r=[r_Tg, r_w["consts"]])
        for j in range(1, 4):
            for h in range(NH):
                _ew("vector", lambda v, h=h, j=j: v.scalar_tensor_tensor(
                    SEf[:, h * 128:(h + 1) * 128], Tg[:, (j * 8 + h) * 128:(j * 8 + h + 1) * 128],
                    ccoef_sb[:, j * 8 + h:j * 8 + h + 1], SEf[:, h * 128:(h + 1) * 128],
                    ALU.mult, ALU.add),
                    w=[r_SEf], r=[r_Tg, r_w["consts"]])
        _ew("scalar", lambda s: s.copy(out=SEb[:], in_=SEf[:]), w=[r_SEb], r=[r_SEf])
        dbg("Tg", [128, 32 * DK], BF16, Tg[:], [r_Tg])
        dbg("SEf", [128, NH * DK], F32, SEf[:], [r_SEf])


    def front_ln(t):
        sl = r_xbt[t % 2]

        def xsrc(kc, t=t):
            return xbt[t % 2][:, kc, :]

        ln_stats(t, xsrc, sl, 1)
        ln_bcast(t, 1)
        ln_apply(lambda kc: x1[:, kc, :], r_x1, xsrc, sl)
        if t + 2 < NT:
            load_x(t + 2)

    def corr_qd(t, n):
        _ew("vector", lambda v, n=n, t=t: v.tensor_tensor(
            qd2[n % 2][:].rearrange("p (h i) -> p h i", i=64),
            qt2[t % 2][:, :, n * 64:(n + 1) * 64],
            cdq8_sb[:, n, :].rearrange("p (h i) -> p h i", i=64), ALU.mult),
            w=[r_qd2[n % 2]], r=[r_qt2[t % 2], r_w["consts"]])

    load_x(0)
    load_x(1)
    front_ln(0)
    proj_q(qt2[0], r_qt2[0])
    for t in range(NT):
        corr_qd(t, 0)
        if t + 1 < NT:
            front_ln(t + 1)
        if t == 0:
            emit_SE()
            proj_q(qt2[1], r_qt2[1])
        for h in range(NH):
            _ew("vector", lambda v, h=h, t=t, sc=float(np.float64(ds[h]) ** (8 * t)):
                v.tensor_scalar_mul(SEbt[:, h * 128:(h + 1) * 128],
                                    SEb[:, h * 128:(h + 1) * 128], sc),
                w=[r_SEbt], r=[r_SEb])
        for n in range(NCH):
            pa = 2 + (n % 2)
            for h in range(NH):
                mm(SEbt[:, h * 128:(h + 1) * 128], qd2[n % 2][:, h * 64:(h + 1) * 64],
                   pa, (slice(0, 128), slice(h * 64, (h + 1) * 64)), True, True,
                   rres=[r_SEbt, r_qd2[n % 2]])
            if n + 1 < NCH:
                corr_qd(t, n + 1)
            _ew("vector", lambda v, n=n, t=t, pa=pa: v.tensor_tensor(
                otc[:, :, t * TT + n * 64:t * TT + (n + 1) * 64],
                pb[pa][:, 0:512].rearrange("p (h i) -> p h i", i=64),
                otc[:, :, t * TT + n * 64:t * TT + (n + 1) * 64], ALU.add),
                w=[r_otc], r=[rpb[pa]])
            if t + 1 < NT and t >= 1:
                proj_q_grp(n, qt2[(t + 1) % 2], r_qt2[(t + 1) % 2])
        for j in range(3):
            s_dma(xf[j][:], xT[j * 128:(j + 1) * 128, t * TT:(t + 1) * TT], wres=[r_xf[j]])
        for oc in range(8):
            bank = oc % 2
            for kc in range(8):
                mm(wo_sb[:, kc, oc * 128:(oc + 1) * 128], otc[:, kc, t * TT:(t + 1) * TT],
                   bank, slice(0, 512), kc == 0, kc == 7, rres=[r_w["wo"], r_otc])
            xsl = oc % 3
            rsl = oc % 2
            _ew("vector", lambda v, bank=bank, xsl=xsl, rsl=rsl: v.tensor_tensor(
                Rf[rsl][:], pb[bank][:, 0:512], xf[xsl][:], ALU.add),
                w=[r_Rf[rsl]], r=[rpb[bank], r_xf[xsl]])
            if oc + 3 < 8:
                s_dma(xf[(oc + 3) % 3][:],
                      xT[(oc + 3) * 128:(oc + 4) * 128, t * TT:(t + 1) * TT],
                      wres=[r_xf[(oc + 3) % 3]])
            ro = r_out.setdefault((t, oc), Res())
            s_dma(outT[oc * 128:(oc + 1) * 128, t * TT:(t + 1) * TT], Rf[rsl][:],
                  wres=[ro], rres=[r_Rf[rsl]])
            _ew("scalar", lambda s, oc=oc, rsl=rsl, t=t: s.copy(
                out=Rb[0][:, oc, :], in_=Rf[rsl][:]),
                w=[r_Rb[0]], r=[r_Rf[rsl]])
        dbg(f"Rb_t{t}", [128, 8, 512], BF16, Rb[0][:], [r_Rb[0]])
        ln_stats(t, lambda kc, t=t: Rb[0][:, kc, :], r_Rb[0], 2)
        ln_bcast(t, 2, m2B, r2B, r_m2B, r_r2B)
        ln_apply(lambda kc, t=t: x2n[:, kc, t * TT:(t + 1) * TT], r_x2n,
                 lambda kc, t=t: Rb[0][:, kc, :], r_Rb[0],
                 m2B, r2B, r_m2B, r_r2B)

    dbg("otc_B", [128, 8, SEG], BF16, otc[:], [r_otc])
    dbg("x2n", [128, 8, SEG], BF16, x2n[:], [r_x2n])

    barB = P.barrier()
    esB.close()
    esA2.close()
    esC = ExitStack()
    NW1, NW2 = 10, 18
    w1p = [sb(esC, f"w1p{i}", [128, HID], BF16) for i in range(NW1)]
    w2p = [sb(esC, f"w2p{i}", [128, 1024], BF16) for i in range(NW2)]
    hgb = sb(esC, "hgb", [128, 16, 512], BF16, side="right")
    hf32 = [sb(esC, f"hf32_{i}", [128, 512], F32) for i in range(2)]
    gateB = [sb(esC, f"gateB{i}", [128, 512], F32) for i in range(2)]
    yacc = sb(esC, "yacc", [128, 8, 1024], F32, side="right")
    of32 = [sb(esC, f"of32_{i}", [128, 512], F32) for i in range(2)]
    xf2 = [sb(esC, f"xf2_{i}", [128, 512], F32) for i in range(2)]
    r_w1p = [Res() for _ in range(NW1)]
    r_w2p = [Res() for _ in range(NW2)]
    r_hgb = [Res() for _ in range(16)]
    r_hf32 = [Res(), Res()]
    r_gateB = [Res(), Res()]
    r_yacc16 = {(oc, t2): Res() for oc in range(8) for t2 in range(2)}

    for c in range(16):
        bank = 6 + (c % 2)
        for kc in range(8):
            mm(x2n[:, kc, c * 128:(c + 1) * 128], rwt_sb[:, kc, :], bank, slice(0, 8),
               kc == 0, kc == 7, rres=[r_x2n, r_w["rwt"]])
        _ew("scalar", lambda s, c=c, bank=bank: s.copy(out=gll[:, c, :], in_=pb[bank][:, 0:8]),
            w=[r_gch], r=[rpb[bank]])

    def vdrain():
        _ew("vector", lambda v: v.drain())

    B8 = [128, 16, 8]
    _ew("vector", lambda v: v.reduce_max(gm1[:], gll[:], axis=AX.X), w=[r_gch], r=[r_gch])
    vdrain()
    _ew("vector", lambda v: v.tensor_tensor(geq[:], gll[:], gm1[:].broadcast_to(B8),
                                            ALU.is_ge), w=[r_gch], r=[r_gch])
    vdrain()
    _ew("vector", lambda v: v.scalar_tensor_tensor(gl3[:], geq[:], -1e30, gll[:],
                                                   ALU.mult, ALU.add), w=[r_gch], r=[r_gch])
    vdrain()
    _ew("vector", lambda v: v.tensor_tensor(gtl[:], gll[:], gm1[:].broadcast_to(B8),
                                            ALU.subtract), w=[r_gch], r=[r_gch])
    vdrain()
    _ew("vector", lambda v: v.reduce_max(gm2[:], gl3[:], axis=AX.X), w=[r_gch], r=[r_gch])
    vdrain()
    _ew("scalar", lambda s: s.activation(out=gp[:], in_=gtl[:], func=ACT.Exp, bias=0.0,
                                         scale=1.0), w=[r_gch], r=[r_gch])
    _ew("vector", lambda v: v.tensor_tensor(gmask[:], gll[:], gm2[:].broadcast_to(B8),
                                            ALU.is_ge), w=[r_gch], r=[r_gch])
    _ew("vector", lambda v: v.tensor_mul(gpm[:], gp[:], gmask[:]), w=[r_gch], r=[r_gch])
    vdrain()
    _ew("vector", lambda v: v.reduce_sum(gZ[:], gpm[:], axis=AX.X), w=[r_gch], r=[r_gch])
    vdrain()
    _ew("vector", lambda v: v.reciprocal(grZ[:], gZ[:]), w=[r_gch], r=[r_gch])
    vdrain()
    _ew("vector", lambda v: v.tensor_tensor(gatb[:], gpm[:], grZ[:].broadcast_to(B8),
                                            ALU.mult), w=[r_gch], r=[r_gch])
    vdrain()

    dbg("gatb", [128, 16, 8], BF16, gatb[:], [r_gch])

    firstg = [True]
    firstv = [True]
    firsts = [True]

    def bar_once(flag):
        if flag[0]:
            flag[0] = False
            return barB
        return []

    sched = [(0, e, e, True) for e in range(E)] + \
            [(1, e, (15 - e if e != 7 else 7), e != 7) for e in [7, 6, 5, 4, 3, 2, 1, 0]]

    def load_w1g(e, gi):
        for kc in range(8):
            slot = (gi * 8 + kc) % NW1
            g_dma(w1p[slot][:], w1t_d[e, kc * 128:(kc + 1) * 128, :], r_w1p[slot],
                  deps=bar_once(firstg))

    def load_w2g(e, gi):
        for hc in range(16):
            slot = (gi * 16 + hc) % NW2
            g_dma(w2p[slot][:], w2t_d[e, hc * 128:(hc + 1) * 128, :], r_w2p[slot],
                  deps=bar_once(firstg))

    load_w1g(0, 0)
    load_w2g(0, 0)
    for si, (half, e, gi, _ld) in enumerate(sched):
        for t2 in range(2):
            t = half * 2 + t2
            gsl = si % 2
            bank_g = 6 + gsl
            for c4 in range(4):
                mm(gatb[:, t * 4 + c4, e:e + 1], eyeb[:], bank_g,
                   (slice(0, 1), slice(c4 * 128, (c4 + 1) * 128)), True, True,
                   rres=[r_gch, r_w["consts"]])
            _ew("vector", lambda v, bank_g=bank_g: v.tensor_copy(
                g1row[:], pb[bank_g][0:1, 0:512]), w=[r_g1], r=[rpb[bank_g]],
                deps=bar_once(firstv))
            mm(ones_f[:], g1row[:], bank_g, slice(0, 512), True, True,
               rres=[r_g1, r_w["consts"]])
            _ew("vector", lambda v, gsl=gsl, bank_g=bank_g: v.tensor_copy(
                gateB[gsl][:], pb[bank_g][:, 0:512]), w=[r_gateB[gsl]], r=[rpb[bank_g]])
            for hc in range(16):
                bank = 2 + (hc % 2)
                for kc in range(8):
                    slot = (gi * 8 + kc) % NW1
                    mm(w1p[slot][:, hc * 128:(hc + 1) * 128],
                       x2n[:, kc, t * TT:(t + 1) * TT],
                       bank, slice(0, 512), kc == 0, kc == 7, rres=[r_w1p[slot], r_x2n])
                hsl = hc % 2
                _ew("scalar", lambda s, hsl=hsl, bank=bank: s.activation(
                    out=hf32[hsl][:], in_=pb[bank][:, 0:512], func=ACT.Gelu,
                    bias=0.0, scale=1.0), w=[r_hf32[hsl]], r=[rpb[bank]],
                    deps=bar_once(firsts))
                _ew("vector", lambda v, hc=hc, hsl=hsl, gsl=gsl: v.tensor_tensor(
                    hgb[:, hc, :], hf32[hsl][:], gateB[gsl][:], ALU.mult),
                    w=[r_hgb[hc]], r=[r_hf32[hsl], r_gateB[gsl]])
            for oc in range(8):
                bank = 4 + (oc % 2)
                for hc in range(16):
                    slot = (gi * 16 + hc) % NW2
                    mm(w2p[slot][:, oc * 128:(oc + 1) * 128], hgb[:, hc, :],
                       bank, slice(0, 512), hc == 0, hc == 15,
                       rres=[r_w2p[slot], r_hgb[hc]])
                if si % 8 == 0:
                    _ew("vector", lambda v, oc=oc, t2=t2, bank=bank: v.tensor_copy(
                        yacc[:, oc, t2 * TT:(t2 + 1) * TT], pb[bank][:, 0:512]),
                        w=[r_yacc16[(oc, t2)]], r=[rpb[bank]])
                else:
                    _ew("vector", lambda v, oc=oc, t2=t2, bank=bank: v.tensor_tensor(
                        yacc[:, oc, t2 * TT:(t2 + 1) * TT], pb[bank][:, 0:512],
                        yacc[:, oc, t2 * TT:(t2 + 1) * TT], ALU.add),
                        w=[r_yacc16[(oc, t2)]], r=[rpb[bank]])
        if si + 1 < len(sched):
            nh, ne, ngi, nld = sched[si + 1]
            if nld:
                load_w1g(ne, ngi)
                load_w2g(ne, ngi)
        if si % 8 != 7:
            continue
        for t2 in range(2):
            t = half * 2 + t2
            for oc in range(8):
                ro = r_out[(t, oc)]
                _ew("gpsimd", lambda g, oc=oc, t2=t2, t=t: g.dma_start(
                    out=outT[oc * 128:(oc + 1) * 128, t * TT:(t + 1) * TT],
                    in_=yacc[:, oc, t2 * TT:(t2 + 1) * TT], accum_op=ALU.add),
                    w=[ro], r=[r_yacc16[(oc, t2)]], dma=True)

    fin = P.cnt["sync"]
    fin_g = P.cnt["gpsimd"]

    with nc.Block() as block:
        sems = sems_h

        @block.tensor
        def _(t):
            for th in P.q["tensor"]:
                th(t, sems)

        @block.vector
        def _(v):
            for th in P.q["vector"]:
                th(v, sems)

        @block.scalar
        def _(s):
            for th in P.q["scalar"]:
                th(s, sems)

        @block.gpsimd
        def _(g):
            for th in P.q["gpsimd"]:
                th(g, sems)

        @block.sync
        def _(s):
            for th in P.q["sync"]:
                th(s, sems)
            s.wait_ge(sems["sync"], fin)
            s.wait_ge(sems["gpsimd"], fin_g)

    return nc


def _consts(s_idx):
    gam = 1.0 - np.exp2(-5.0 - np.arange(NH, dtype=np.float64))
    idx = np.arange(CH)
    cdt = np.zeros((128, 512), np.float64)
    for h in range(NH):
        diff = idx[None, :] - idx[:, None]
        m = np.where(diff >= 0, gam[h] ** np.maximum(diff, 0), 0.0)
        cdt[0:64, h * 64:(h + 1) * 64] = m
        cdt[64:128, h * 64:(h + 1) * 64] = m
    cdq = np.zeros((128, 512), np.float64)
    for h in range(NH):
        cdq[:, h * 64:(h + 1) * 64] = (gam[h] ** (idx + 1))[None, :]
    cdq8 = np.zeros((128, 8, 512), np.float64)
    for n in range(8):
        for h in range(NH):
            cdq8[:, n, h * 64:(h + 1) * 64] = (gam[h] ** (idx + 1) * gam[h] ** (CH * n))[None, :]
    cdk = np.zeros((128, 8), np.float64)
    pos = np.arange(128) % 64
    for h in range(NH):
        cdk[:, h] = gam[h] ** (CH - 1 - pos)
    ccoef = np.zeros((128, 32), np.float64)
    for j in range(4):
        for h in range(NH):
            if j < s_idx:
                ccoef[:, j * 8 + h] = gam[h] ** (CH * 32 * (s_idx - 1 - j))
    return (cdt.astype(np.float32), cdq.astype(np.float32),
            cdk.astype(np.float32), ccoef.astype(np.float32), cdq8.astype(np.float32))


def kernel(L0_attn_hidden_states, L0_attn_rel_pos, L0_attn_Wq, L0_attn_Wk,
           L0_attn_Wv, L0_attn_Wo, L0_ffn_router_w, L0_ffn_w1, L0_ffn_w2):
    x = np.ascontiguousarray(np.asarray(L0_attn_hidden_states, dtype=np.float32))
    wq = np.ascontiguousarray(np.asarray(L0_attn_Wq, dtype=np.float32))
    wk = np.ascontiguousarray(np.asarray(L0_attn_Wk, dtype=np.float32))
    wv = np.ascontiguousarray(np.asarray(L0_attn_Wv, dtype=np.float32))
    wo = np.ascontiguousarray(np.asarray(L0_attn_Wo, dtype=np.float32))
    rw = np.asarray(L0_ffn_router_w, dtype=np.float32)
    w1 = np.asarray(L0_ffn_w1, dtype=np.float32)
    w2 = np.asarray(L0_ffn_w2, dtype=np.float32)

    xT = np.ascontiguousarray(np.transpose(x, (0, 2, 1)))
    w1t = np.ascontiguousarray(np.transpose(w1, (0, 2, 1)))
    w2t = np.ascontiguousarray(np.transpose(w2, (0, 2, 1)))
    rwt = np.ascontiguousarray(rw.T)
    eye = np.eye(128, dtype=np.float32)

    nc = build()
    in_maps = []
    for c in range(8):
        b, s = c // 4, c % 4
        cdt, cdq, cdk, ccoef, cdq8 = _consts(s)
        in_maps.append({
            "xT": np.ascontiguousarray(xT[b, :, s * SEG:(s + 1) * SEG]),
            "wq": wq, "wk": wk, "wv": wv, "wo": wo,
            "w1t": w1t, "w2t": w2t, "rwt": rwt, "eye": eye,
            "cdt": cdt, "cdq": cdq, "cdk": cdk, "ccoef": ccoef, "cdq8": cdq8,
        })
    res = run_bass_kernel_spmd(nc, in_maps, core_ids=list(range(8)))
    out = np.empty((2, 8192, 1024), np.float32)
    for c in range(8):
        b, s = c // 4, c % 4
        out[b, s * SEG:(s + 1) * SEG, :] = res.results[c]["outT"].T
    return out


# version 27
# speedup vs baseline: 1.0385x; 1.0385x over previous
import numpy as np
import concourse.bass as bass
import concourse.mybir as mybir
from concourse.bass_utils import run_bass_kernel_spmd
from contextlib import ExitStack

F32 = mybir.dt.float32
BF16 = mybir.dt.bfloat16
AX = mybir.AxisListType
ALU = mybir.AluOpType
ACT = mybir.ActivationFunctionType

NH = 8
DK = 128
CH = 64
SEG = 2048
NT = 4
TT = 512
NCH = 8
D = 1024
E = 8
HID = 2048
QSC = float(DK) ** -0.5

DBG_ON = False

ENGINES = ["tensor", "vector", "scalar", "gpsimd", "sync", "cc"]
RUN_ENGINES = ["tensor", "vector", "scalar", "gpsimd", "sync"]


class Prog:
    def __init__(self):
        self.q = {e: [] for e in RUN_ENGINES}
        self.cnt = {e: 0 for e in ENGINES}
        self.wm = {e: {p: 0 for p in ENGINES} for e in ENGINES}

    def emit(self, eng, fn, dma=False, deps=(), inc_eng=None, inc=None):
        if inc is None:
            inc = 16 if dma else 1
        if inc_eng is None:
            inc_eng = eng
        self.cnt[inc_eng] += inc
        target = self.cnt[inc_eng]
        best = {}
        for (de, dv) in deps:
            if (de == eng and de not in ("gpsimd", "sync")) or dv <= 0:
                continue
            if dv > best.get(de, 0):
                best[de] = dv
        wm = self.wm[eng]
        final = [(de, dv) for de, dv in best.items() if dv > wm[de]]
        for de, dv in final:
            wm[de] = dv

        def thunk(engine, sems, final=final, fn=fn, inc_eng=inc_eng, inc=inc):
            for (de, dv) in final:
                engine.wait_ge(sems[de], dv)
            fn(engine).then_inc(sems[inc_eng], inc)

        self.q[eng].append(thunk)
        return (inc_eng, target)

    def dep(self, eng):
        return (eng, self.cnt[eng])

    def barrier(self, exclude=()):
        return [self.dep(e) for e in ENGINES if e not in exclude]


class Res:
    __slots__ = ("w", "r")

    def __init__(self):
        self.w = []
        self.r = []

    def wdeps(self):
        return self.r + self.w

    def rdeps(self):
        return list(self.w)

    def wrote(self, d):
        self.w = [d]
        self.r = []

    def read(self, d):
        self.r.append(d)


def build():
    nc = bass.Bass("TRN2", target_bir_lowering=False)
    P = Prog()

    def _ew(eng, fn, w=(), r=(), deps=(), dma=False, inc_eng=None, inc=None):
        dd = list(deps)
        for res in w:
            dd += res.wdeps()
        for res in r:
            dd += res.rdeps()
        d = P.emit(eng, fn, dma=dma, deps=dd, inc_eng=inc_eng, inc=inc)
        for res in w:
            res.wrote(d)
        for res in r:
            res.read(d)
        return d

    xT = nc.dram_tensor("xT", [D, SEG], F32, kind="ExternalInput").ap()
    wq_d = nc.dram_tensor("wq", [D, D], F32, kind="ExternalInput").ap()
    wk_d = nc.dram_tensor("wk", [D, D], F32, kind="ExternalInput").ap()
    wv_d = nc.dram_tensor("wv", [D, D], F32, kind="ExternalInput").ap()
    wo_d = nc.dram_tensor("wo", [D, D], F32, kind="ExternalInput").ap()
    w1t_d = nc.dram_tensor("w1t", [E, D, HID], F32, kind="ExternalInput").ap()
    w2t_d = nc.dram_tensor("w2t", [E, HID, D], F32, kind="ExternalInput").ap()
    rwt_d = nc.dram_tensor("rwt", [D, E], F32, kind="ExternalInput").ap()
    eye_d = nc.dram_tensor("eye", [128, 128], F32, kind="ExternalInput").ap()
    cdt_d = nc.dram_tensor("cdt", [128, 512], F32, kind="ExternalInput").ap()
    cdq_d = nc.dram_tensor("cdq", [128, 512], F32, kind="ExternalInput").ap()
    cdq8_d = nc.dram_tensor("cdq8", [128, 8, 512], F32, kind="ExternalInput").ap()
    cdk_d = nc.dram_tensor("cdk", [128, 8], F32, kind="ExternalInput").ap()
    ccoef_d = nc.dram_tensor("ccoef", [128, 32], F32, kind="ExternalInput").ap()
    outT = nc.dram_tensor("outT", [D, SEG], F32, kind="ExternalOutput").ap()
    dbg_t = {}

    def dbg(name, shape, dt, src_ap, rres, deps=()):
        if not DBG_ON:
            return
        if name not in dbg_t:
            dbg_t[name] = nc.dram_tensor("dbg_" + name, shape, dt, kind="ExternalOutput").ap()
        _ew("sync", lambda s, a=dbg_t[name], b=src_ap: s.dma_start(out=a, in_=b),
            r=list(rres), deps=deps, dma=True)

    cc_in = nc.dram_tensor("cc_in", [NH * DK, DK], F32)
    cc_out = nc.dram_tensor("cc_out", [4 * NH * DK, DK], F32)

    ds = [float(1.0 - 2.0 ** (-5.0 - h)) ** CH for h in range(NH)]

    es0 = ExitStack()
    esA1 = ExitStack()
    esA2 = ExitStack()
    esB = ExitStack()

    def sb(es, name, shape, dt, side="left"):
        return es.enter_context(nc.sbuf_tensor(name, shape, dt, side=side))

    cdt_sb = sb(es0, "cdt_sb", [128, 512], BF16)
    cdq_sb = sb(es0, "cdq_sb", [128, 512], BF16)
    cdk_sb = sb(es0, "cdk_sb", [128, 8], F32)
    ccoef_sb = sb(es0, "ccoef_sb", [128, 32], F32)
    eyeb = sb(es0, "eyeb", [128, 128], BF16)
    rwt_sb = sb(es0, "rwt_sb", [128, 8, 8], BF16)
    ones_b = sb(es0, "ones_b", [128, 1], BF16)
    ones1b = sb(es0, "ones1b", [1, 128], BF16)
    ones_f = sb(es0, "ones_f", [1, 128], F32)
    mrow = sb(es0, "mrow", [1, 1024], F32)
    mrow2 = sb(es0, "mrow2", [1, 1024], F32)
    tmprow = sb(es0, "tmprow", [1, 1024], F32)
    gll = sb(es0, "gll", [128, 16, 8], F32)
    gm1 = sb(es0, "gm1", [128, 16, 1], F32)
    geq = sb(es0, "geq", [128, 16, 8], F32)
    gl3 = sb(es0, "gl3", [128, 16, 8], F32)
    gm2 = sb(es0, "gm2", [128, 16, 1], F32)
    gtl = sb(es0, "gtl", [128, 16, 8], F32)
    gp = sb(es0, "gp", [128, 16, 8], F32)
    gmask = sb(es0, "gmask", [128, 16, 8], F32)
    gpm = sb(es0, "gpm", [128, 16, 8], F32)
    gZ = sb(es0, "gZ", [128, 16, 1], F32)
    grZ = sb(es0, "grZ", [128, 16, 1], F32)
    gatb = sb(es0, "gatb", [128, 16, 8], BF16)
    g1row = sb(es0, "g1row", [1, 512], F32)

    xbt = [sb(esA2, f"xbt{i}", [128, 8, 512], BF16) for i in range(2)]
    otc = sb(esA2, "otc", [128, 8, SEG], BF16)
    x1 = sb(esA2, "x1", [128, 8, 512], BF16)
    qt2 = [sb(esA2, f"qt{i}", [128, 8, 512], BF16) for i in range(2)]
    qt = qt2[0]
    qd2 = [sb(esA2, f"qd{i}", [128, 512], BF16) for i in range(2)]
    wq_sb = sb(esA2, "wq_sb", [128, 8, 1024], BF16)
    mB = sb(esA2, "mB", [128, 512], BF16)
    rB = sb(esA2, "rB", [128, 512], BF16)
    m2B = sb(esA2, "m2B", [128, 512], BF16)
    r2B = sb(esA2, "r2B", [128, 512], BF16)
    sqb = sb(esA2, "sqb", [128, 512], BF16)
    Sf = sb(esA1, "Sf", [128, NH * DK], F32, side="right")
    Sb = sb(esA1, "Sb", [128, NH * DK], BF16, side="right")
    wk_sb = sb(esA1, "wk_sb", [128, 8, 1024], BF16, side="right")
    wv_sb = sb(esA1, "wv_sb", [128, 8, 1024], BF16, side="right")
    kfm2 = [sb(esA1, f"kfm{i}", [128, 8, 512], BF16, side="right") for i in range(2)]
    kd2 = [sb(esA1, f"kd{i}", [128, 4, 1024], BF16, side="right") for i in range(2)]
    vtk2 = [sb(esA1, f"vtk{i}", [128, 4, 1024], BF16, side="right") for i in range(2)]
    kfm = kfm2[0]
    kd = kd2[0]
    vtk = vtk2[0]
    sT = sb(esA1, "sT", [128, 512], BF16, side="right")

    pb = [es0.enter_context(nc.psum_tensor(f"pb{i}", [128, 512], F32)) for i in range(8)]
    rpb = [Res() for _ in range(8)]

    sems_h = {}
    for e in ENGINES:
        sems_h[e] = es0.enter_context(nc.semaphore(f"sem_{e}"))

    r_xbt = [Res(), Res()]
    r_otc = Res()
    r_x1 = Res()
    r_qt2 = [Res(), Res()]
    r_qt = r_qt2[0]
    r_qd2 = [Res(), Res()]
    r_kfm2 = [Res(), Res()]
    r_kd2 = [Res(), Res()]
    r_vtk2 = [Res(), Res()]
    r_kfm = r_kfm2[0]
    r_kd = r_kd2[0]
    r_vtk = r_vtk2[0]
    r_sT = Res()
    r_Sf = [Res() for _ in range(NH)]
    r_Sb = [Res() for _ in range(NH)]
    r_SEf = Res()
    r_SEb = Res()
    r_mB = Res()
    r_rB = Res()
    r_m2B = Res()
    r_r2B = Res()
    r_sqb = Res()
    r_mrow = Res()
    r_mrow2 = Res()
    r_tmprow = Res()
    r_x2n = Res()
    r_gch = Res()
    r_g1 = Res()
    r_w = {n: Res() for n in ["wq", "wk", "wv", "wo", "rwt", "consts"]}
    r_out = {}

    def mm(lhsT, rhs, bank, cols, start, stop, deps=(), rres=(), tile_position=None):
        if isinstance(cols, tuple):
            out = pb[bank][cols[0], cols[1]]
        else:
            out = pb[bank][:, cols]
        dd = list(deps)
        if start:
            dd += rpb[bank].wdeps()
        for res in rres:
            dd += res.rdeps()
        d = P.emit("tensor", lambda t, o=out, l=lhsT, r=rhs, s1=start, s2=stop, tp=tile_position:
                   t.matmul(o, l, r, start=s1, stop=s2, tile_position=tp), deps=dd)
        rpb[bank].wrote(d)
        for res in rres:
            res.read(d)
        return d

    def g_dma(dst, src, res, deps=()):
        return _ew("gpsimd", lambda g, a=dst, b=src: g.dma_start(out=a, in_=b),
                   w=[res], deps=deps, dma=True)

    def s_dma(dst, src, wres=(), rres=(), deps=()):
        return _ew("sync", lambda s, a=dst, b=src: s.dma_start(out=a, in_=b),
                   w=wres, r=rres, deps=deps, dma=True)

    wq_r = wq_d.rearrange("(k p) o -> p k o", p=128)
    wk_r = wk_d.rearrange("(k p) o -> p k o", p=128)
    wv_r = wv_d.rearrange("(k p) o -> p k o", p=128)
    wo_r = wo_d.rearrange("(k p) o -> p k o", p=128)
    xT_r = xT.rearrange("(k p) t -> p k t", p=128)

    _ew("vector", lambda v: v.memset(ones_b[:], 1.0), w=[r_w["consts"]])
    _ew("vector", lambda v: v.memset(ones1b[:], 1.0), w=[r_w["consts"]])
    _ew("vector", lambda v: v.memset(ones_f[:], 1.0), w=[r_w["consts"]])
    _ew("vector", lambda v: v.memset(Sf[:], 0.0), w=r_Sf)
    _ew("vector", lambda v: v.memset(Sb[:], 0.0), w=r_Sb)
    g_dma(eyeb[:], eye_d, r_w["consts"])
    g_dma(cdt_sb[:], cdt_d, r_w["consts"])
    g_dma(cdq_sb[:], cdq_d, r_w["consts"])
    s_dma(cdk_sb[:], cdk_d, wres=[r_w["consts"]])
    s_dma(ccoef_sb[:], ccoef_d, wres=[r_w["consts"]])
    g_dma(rwt_sb[:], rwt_d.rearrange("(k p) e -> p k e", p=128), r_w["rwt"])
    g_dma(wk_sb[:], wk_r, r_w["wk"])
    g_dma(wv_sb[:], wv_r, r_w["wv"])
    g_dma(wq_sb[:], wq_r, r_w["wq"])

    def ln_stats(t, src3, res_src, which, extra=()):
        mrow_t = mrow if which == 1 else mrow2
        r_mr = r_mrow if which == 1 else r_mrow2
        for kc in range(8):
            mm(ones_b[:], src3(kc), 0, (slice(0, 1), slice(0, 512)), kc == 0, kc == 7,
               rres=[res_src, r_w["consts"]], deps=list(extra) if kc == 0 else ())
            _ew("vector", lambda v, kc=kc: v.tensor_mul(sqb[:], src3(kc), src3(kc)),
                w=[r_sqb], r=[res_src])
            mm(ones_b[:], sqb[:], 1, (slice(0, 1), slice(0, 512)), kc == 0, kc == 7,
               rres=[r_sqb, r_w["consts"]])
        base = 0
        m_ap = mrow_t[0:1, base:base + 512]
        r_ap = mrow_t[0:1, base + 512:base + 1024]
        _ew("vector", lambda v: v.tensor_scalar_mul(m_ap, pb[0][0:1, 0:512], 1.0 / D),
            w=[r_mr], r=[rpb[0]])
        _ew("vector", lambda v: v.tensor_scalar_mul(tmprow[0:1, 0:512], pb[1][0:1, 0:512], 1.0 / D),
            w=[r_tmprow], r=[rpb[1]])
        _ew("vector", lambda v: v.drain())
        _ew("vector", lambda v: v.tensor_mul(tmprow[0:1, 512:1024], m_ap, m_ap),
            w=[r_tmprow], r=[r_mr])
        _ew("vector", lambda v: v.drain())
        _ew("vector", lambda v: v.scalar_tensor_tensor(
            g1row[0:1, 0:512], tmprow[0:1, 512:1024], -1.0, tmprow[0:1, 0:512],
            ALU.mult, ALU.add), w=[r_g1], r=[r_tmprow])
        _ew("vector", lambda v: v.drain())
        _ew("vector", lambda v: v.tensor_scalar_add(tmprow[0:1, 0:512], g1row[0:1, 0:512],
                                                    1e-5), w=[r_tmprow], r=[r_g1])
        _ew("vector", lambda v: v.drain())
        _ew("scalar", lambda s: s.activation(out=tmprow[0:1, 512:1024], in_=tmprow[0:1, 0:512],
                                             func=ACT.Ln, bias=0.0, scale=1.0),
            w=[r_tmprow], r=[r_tmprow])
        _ew("scalar", lambda s: s.activation(out=r_ap, in_=tmprow[0:1, 512:1024],
                                             func=ACT.Exp, bias=0.0, scale=-0.5),
            w=[r_mr], r=[r_tmprow])

    def ln_bcast(t, which, bm=None, br=None, rm=None, rr=None):
        mrow_t = mrow if which == 1 else mrow2
        r_mr = r_mrow if which == 1 else r_mrow2
        bm = bm if bm is not None else mB
        br = br if br is not None else rB
        rm = rm if rm is not None else r_mB
        rr = rr if rr is not None else r_rB
        mm(ones_f[:], mrow_t[0:1, 0:512], 0, slice(0, 512), True, True,
           rres=[r_mr, r_w["consts"]])
        _ew("vector", lambda v: v.tensor_copy(bm[:], pb[0][:, 0:512]), w=[rm], r=[rpb[0]])
        mm(ones_f[:], mrow_t[0:1, 512:1024], 1, slice(0, 512), True, True,
           rres=[r_mr, r_w["consts"]])
        _ew("vector", lambda v: v.tensor_copy(br[:], pb[1][:, 0:512]), w=[rr], r=[rpb[1]])

    def ln_apply(dst3, res_dst, src3, res_src, bm=None, br=None, rm=None, rr=None):
        bm = bm if bm is not None else mB
        br = br if br is not None else rB
        rm = rm if rm is not None else r_mB
        rr = rr if rr is not None else r_rB
        for kc in range(8):
            _ew("vector", lambda v, kc=kc: v.tensor_tensor(sqb[:], src3(kc), bm[:], ALU.subtract),
                w=[r_sqb], r=[res_src, rm])
            _ew("vector", lambda v, kc=kc: v.tensor_tensor(dst3(kc), sqb[:], br[:], ALU.mult),
                w=[res_dst], r=[r_sqb, rr])

    def proj_q_grp(jc, dst, rdst):
        bank = jc % 2
        for kc in range(8):
            mm(wq_sb[:, kc, jc * 128:(jc + 1) * 128], x1[:, kc, :], bank, slice(0, 512),
               kc == 0, kc == 7, rres=[r_w["wq"], r_x1])
        _ew("scalar", lambda s, jc=jc, bank=bank, dst=dst: s.activation(
            out=dst[:, jc, :], in_=pb[bank][:, 0:512], func=ACT.Copy, bias=0.0, scale=QSC),
            w=[rdst], r=[rpb[bank]])

    def proj_q(dst=None, rdst=None):
        dst = dst if dst is not None else qt
        rdst = rdst if rdst is not None else r_qt
        for jc in range(8):
            proj_q_grp(jc, dst, rdst)

    def load_x(t):
        g_dma(xbt[t % 2][:], xT_r[:, :, t * TT:(t + 1) * TT], r_xbt[t % 2])

    load_x(0)
    load_x(1)
    for t in range(NT):
        sl = r_xbt[t % 2]
        kfm_, kd_, vtk_ = kfm2[t % 2], kd2[t % 2], vtk2[t % 2]
        r_kfm_, r_kd_, r_vtk_ = r_kfm2[t % 2], r_kd2[t % 2], r_vtk2[t % 2]

        def xsrc(kc, t=t):
            return xbt[t % 2][:, kc, :]

        tb = [P.dep("tensor"), P.dep("vector"), P.dep("scalar")] if t > 0 else []
        ln_stats(t, xsrc, sl, 1, extra=tb)
        ln_bcast(t, 1)
        ln_apply(lambda kc: x1[:, kc, :], r_x1, xsrc, sl)
        if t + 2 < NT:
            load_x(t + 2)
        proj_q(qt2[t % 2], r_qt2[t % 2])
        for jc in range(8):
            bank = jc % 2
            for kc in range(8):
                mm(wk_sb[:, kc, jc * 128:(jc + 1) * 128], x1[:, kc, :], bank, slice(0, 512),
                   kc == 0, kc == 7, rres=[r_w["wk"], r_x1])
            _ew("scalar", lambda s, jc=jc, bank=bank, kf=kfm_: s.copy(out=kf[:, jc, :],
                                                             in_=pb[bank][:, 0:512]),
                w=[r_kfm_], r=[rpb[bank]])
        for c4 in range(4):
            for half in range(2):
                bank = (c4 * 2 + half) % 2
                for kc in range(8):
                    mm(x1[:, kc, c4 * 128:(c4 + 1) * 128],
                       wk_sb[:, kc, half * 512:(half + 1) * 512],
                       bank, slice(0, 512), kc == 0, kc == 7, rres=[r_x1, r_w["wk"]])
                for h4 in range(4):
                    h = half * 4 + h4
                    _ew("vector", lambda v, c4=c4, h=h, h4=h4, bank=bank, kdx=kd_: v.tensor_scalar(
                        kdx[:, c4, h * 128:(h + 1) * 128], pb[bank][:, h4 * 128:(h4 + 1) * 128],
                        cdk_sb[:, h:h + 1], None, ALU.mult),
                        w=[r_kd_], r=[rpb[bank], r_w["consts"]])
                bankv = 1 - bank
                for kc in range(8):
                    mm(x1[:, kc, c4 * 128:(c4 + 1) * 128],
                       wv_sb[:, kc, half * 512:(half + 1) * 512],
                       bankv, slice(0, 512), kc == 0, kc == 7, rres=[r_x1, r_w["wv"]])
                _ew("vector", lambda v, c4=c4, half=half, bankv=bankv, vt=vtk_: v.tensor_copy(
                    vt[:, c4, half * 512:(half + 1) * 512], pb[bankv][:, 0:512]),
                    w=[r_vtk_], r=[rpb[bankv]])
        for n in range(NCH):
            c4, sub = n // 2, n % 2
            pa = 2 + (n % 2)
            po = 4 + (n % 2)
            _ew("vector", lambda v, n=n, qx=qt2[t % 2]: v.tensor_tensor(
                qd2[n % 2][:].rearrange("p (h i) -> p h i", i=64),
                qx[:, :, n * 64:(n + 1) * 64],
                cdq_sb[:].rearrange("p (h i) -> p h i", i=64), ALU.mult),
                w=[r_qd2[n % 2]], r=[r_qt2[t % 2], r_w["consts"]])
            pl = slice(sub * 64, (sub + 1) * 64)
            for h in range(NH):
                mm(kfm_[:, h, n * 64:(n + 1) * 64], qt2[t % 2][:, h, n * 64:(n + 1) * 64],
                   pa, (pl, slice(h * 64, (h + 1) * 64)), True, True,
                   rres=[r_kfm_, r_qt2[t % 2]], tile_position=(0, sub * 64))
            _ew("vector", lambda v, pa=pa, pl=pl: v.tensor_tensor(sT[pl, :], pb[pa][pl, 0:512],
                                                                  cdt_sb[pl, :], ALU.mult),
                w=[r_sT], r=[rpb[pa], r_w["consts"]])
            for h in range(NH):
                mm(vtk_[pl, c4, h * 128:(h + 1) * 128],
                   sT[pl, h * 64:(h + 1) * 64],
                   po, (slice(0, 128), slice(h * 64, (h + 1) * 64)), True, False,
                   rres=[r_vtk_, r_sT])
                mm(Sb[:, h * 128:(h + 1) * 128], qd2[n % 2][:, h * 64:(h + 1) * 64],
                   po, (slice(0, 128), slice(h * 64, (h + 1) * 64)), False, True,
                   rres=[r_Sb[h], r_qd2[n % 2]])
            for h in range(NH):
                _ew("scalar", lambda s, h=h, t=t, n=n, po=po: s.copy(
                    out=otc[:, h, t * TT + n * 64:t * TT + (n + 1) * 64],
                    in_=pb[po][:, h * 64:(h + 1) * 64]),
                    w=[r_otc], r=[rpb[po]])
            for h in range(NH):
                pA = 6 + (h % 2)
                mm(kd_[sub * 64:(sub + 1) * 64, c4, h * 128:(h + 1) * 128],
                   vtk_[sub * 64:(sub + 1) * 64, c4, h * 128:(h + 1) * 128],
                   pA, (slice(0, 128), slice((h // 2) * 128, (h // 2 + 1) * 128)), True, True,
                   rres=[r_kd_, r_vtk_])
                _ew("vector", lambda v, h=h, pA=pA: v.scalar_tensor_tensor(
                    Sf[:, h * 128:(h + 1) * 128], Sf[:, h * 128:(h + 1) * 128], ds[h],
                    pb[pA][:, (h // 2) * 128:(h // 2 + 1) * 128], ALU.mult, ALU.add),
                    w=[r_Sf[h]], r=[rpb[pA]])
                _ew("scalar", lambda s, h=h: s.copy(out=Sb[:, h * 128:(h + 1) * 128],
                                                    in_=Sf[:, h * 128:(h + 1) * 128]),
                    w=[r_Sb[h]], r=[r_Sf[h]])
        if t == 0:
            dbg("x1_t0", [128, 8, 512], BF16, x1[:], [r_x1])
            dbg("qt_t0", [128, 8, 512], BF16, qt[:], [r_qt])
            dbg("kfm_t0", [128, 8, 512], BF16, kfm[:], [r_kfm])
            dbg("kd_t0", [128, 4, 1024], BF16, kd[:], [r_kd])
            dbg("vtk_t0", [128, 4, 1024], BF16, vtk[:], [r_vtk])
            dbg("mrow_t0", [1, 1024], F32, mrow[:], [r_mrow])

    dbg("Sf_A", [128, NH * DK], F32, Sf[:], r_Sf)

    cc_in_r = cc_in.ap().rearrange("(h p) d -> p h d", p=128)
    d_ccin = s_dma(cc_in_r, Sf[:].rearrange("p (h d) -> p h d", d=DK), rres=r_Sf)
    d_cc = P.emit("gpsimd", lambda g: g.collective_compute(
        "AllGather", ALU.bypass, replica_groups=[[0, 1, 2, 3], [4, 5, 6, 7]],
        ins=[cc_in.ap()], outs=[cc_out.ap()]), deps=[d_ccin], inc_eng="cc", inc=1)

    barA = P.barrier(exclude=('cc',))
    esA1.close()
    x2n = sb(es0, "x2n", [128, 8, SEG], BF16, side="right")
    Tg = sb(esB, "Tg", [128, 32 * DK], BF16)
    wo_sb = sb(esB, "wo_sb", [128, 8, 1024], BF16)
    Rf = [sb(esB, f"Rf{i}", [128, 512], F32) for i in range(2)]
    Rb = [sb(esB, "Rb0", [128, 8, 512], BF16)]
    cdq8_sb = sb(esB, "cdq8_sb", [128, 8, 512], BF16)
    SEbt = sb(esB, "SEbt", [128, NH * DK], BF16)
    xf = [sb(esB, f"xf{i}", [128, 512], F32) for i in range(3)]
    SEf = sb(esB, "SEf", [128, NH * DK], F32)
    SEb = sb(esB, "SEb", [128, NH * DK], BF16)
    r_Tg = Res()
    r_Rf = [Res(), Res()]
    r_Rb = [Res()]
    r_SEbt = Res()
    r_xf = [Res() for _ in range(3)]

    g_dma(wo_sb[:], wo_r, r_w["wo"], deps=barA)
    g_dma(cdq8_sb[:], cdq8_d, r_w["consts"], deps=barA)
    def emit_SE():
        cc_out_r = cc_out.ap().rearrange("(j p) d -> p j d", p=128)
        g_dma(Tg[:].rearrange("p (j d) -> p j d", d=DK), cc_out_r, r_Tg,
              deps=barA + [d_cc])
        for h in range(NH):
            _ew("vector", lambda v, h=h: v.tensor_scalar(
                SEf[:, h * 128:(h + 1) * 128], Tg[:, h * 128:(h + 1) * 128],
                ccoef_sb[:, h:h + 1], None, ALU.mult),
                w=[r_SEf], r=[r_Tg, r_w["consts"]])
        for j in range(1, 4):
            for h in range(NH):
                _ew("vector", lambda v, h=h, j=j: v.scalar_tensor_tensor(
                    SEf[:, h * 128:(h + 1) * 128], Tg[:, (j * 8 + h) * 128:(j * 8 + h + 1) * 128],
                    ccoef_sb[:, j * 8 + h:j * 8 + h + 1], SEf[:, h * 128:(h + 1) * 128],
                    ALU.mult, ALU.add),
                    w=[r_SEf], r=[r_Tg, r_w["consts"]])
        _ew("scalar", lambda s: s.copy(out=SEb[:], in_=SEf[:]), w=[r_SEb], r=[r_SEf])
        dbg("Tg", [128, 32 * DK], BF16, Tg[:], [r_Tg])
        dbg("SEf", [128, NH * DK], F32, SEf[:], [r_SEf])


    def front_ln(t):
        sl = r_xbt[t % 2]

        def xsrc(kc, t=t):
            return xbt[t % 2][:, kc, :]

        ln_stats(t, xsrc, sl, 1)
        ln_bcast(t, 1)
        ln_apply(lambda kc: x1[:, kc, :], r_x1, xsrc, sl)
        if t + 2 < NT:
            load_x(t + 2)

    def corr_qd(t, n):
        _ew("vector", lambda v, n=n, t=t: v.tensor_tensor(
            qd2[n % 2][:].rearrange("p (h i) -> p h i", i=64),
            qt2[t % 2][:, :, n * 64:(n + 1) * 64],
            cdq8_sb[:, n, :].rearrange("p (h i) -> p h i", i=64), ALU.mult),
            w=[r_qd2[n % 2]], r=[r_qt2[t % 2], r_w["consts"]])

    load_x(0)
    load_x(1)
    front_ln(0)
    proj_q(qt2[0], r_qt2[0])
    for t in range(NT):
        corr_qd(t, 0)
        if t + 1 < NT:
            front_ln(t + 1)
        if t == 0:
            emit_SE()
            proj_q(qt2[1], r_qt2[1])
        for h in range(NH):
            _ew("vector", lambda v, h=h, t=t, sc=float(np.float64(ds[h]) ** (8 * t)):
                v.tensor_scalar_mul(SEbt[:, h * 128:(h + 1) * 128],
                                    SEb[:, h * 128:(h + 1) * 128], sc),
                w=[r_SEbt], r=[r_SEb])
        for n in range(NCH):
            pa = 2 + (n % 2)
            for h in range(NH):
                mm(SEbt[:, h * 128:(h + 1) * 128], qd2[n % 2][:, h * 64:(h + 1) * 64],
                   pa, (slice(0, 128), slice(h * 64, (h + 1) * 64)), True, True,
                   rres=[r_SEbt, r_qd2[n % 2]])
            if n + 1 < NCH:
                corr_qd(t, n + 1)
            _ew("vector", lambda v, n=n, t=t, pa=pa: v.tensor_tensor(
                otc[:, :, t * TT + n * 64:t * TT + (n + 1) * 64],
                pb[pa][:, 0:512].rearrange("p (h i) -> p h i", i=64),
                otc[:, :, t * TT + n * 64:t * TT + (n + 1) * 64], ALU.add),
                w=[r_otc], r=[rpb[pa]])
            if t + 1 < NT and t >= 1:
                proj_q_grp(n, qt2[(t + 1) % 2], r_qt2[(t + 1) % 2])
        for j in range(3):
            s_dma(xf[j][:], xT[j * 128:(j + 1) * 128, t * TT:(t + 1) * TT], wres=[r_xf[j]])
        for oc in range(8):
            bank = oc % 2
            for kc in range(8):
                mm(wo_sb[:, kc, oc * 128:(oc + 1) * 128], otc[:, kc, t * TT:(t + 1) * TT],
                   bank, slice(0, 512), kc == 0, kc == 7, rres=[r_w["wo"], r_otc])
            xsl = oc % 3
            rsl = oc % 2
            _ew("vector", lambda v, bank=bank, xsl=xsl, rsl=rsl: v.tensor_tensor(
                Rf[rsl][:], pb[bank][:, 0:512], xf[xsl][:], ALU.add),
                w=[r_Rf[rsl]], r=[rpb[bank], r_xf[xsl]])
            if oc + 3 < 8:
                s_dma(xf[(oc + 3) % 3][:],
                      xT[(oc + 3) * 128:(oc + 4) * 128, t * TT:(t + 1) * TT],
                      wres=[r_xf[(oc + 3) % 3]])
            ro = r_out.setdefault((t, oc), Res())
            s_dma(outT[oc * 128:(oc + 1) * 128, t * TT:(t + 1) * TT], Rf[rsl][:],
                  wres=[ro], rres=[r_Rf[rsl]])
            _ew("scalar", lambda s, oc=oc, rsl=rsl, t=t: s.copy(
                out=Rb[0][:, oc, :], in_=Rf[rsl][:]),
                w=[r_Rb[0]], r=[r_Rf[rsl]])
        dbg(f"Rb_t{t}", [128, 8, 512], BF16, Rb[0][:], [r_Rb[0]])
        ln_stats(t, lambda kc, t=t: Rb[0][:, kc, :], r_Rb[0], 2)
        ln_bcast(t, 2, m2B, r2B, r_m2B, r_r2B)
        ln_apply(lambda kc, t=t: x2n[:, kc, t * TT:(t + 1) * TT], r_x2n,
                 lambda kc, t=t: Rb[0][:, kc, :], r_Rb[0],
                 m2B, r2B, r_m2B, r_r2B)

    dbg("otc_B", [128, 8, SEG], BF16, otc[:], [r_otc])
    dbg("x2n", [128, 8, SEG], BF16, x2n[:], [r_x2n])

    barB = P.barrier()
    esB.close()
    esA2.close()
    esC = ExitStack()
    NW1, NW2 = 10, 18
    w1p = [sb(esC, f"w1p{i}", [128, HID], BF16) for i in range(NW1)]
    w2p = [sb(esC, f"w2p{i}", [128, 1024], BF16) for i in range(NW2)]
    hgb = sb(esC, "hgb", [128, 16, 512], BF16, side="right")
    hf32 = [sb(esC, f"hf32_{i}", [128, 512], F32) for i in range(2)]
    gateB = [sb(esC, f"gateB{i}", [128, 512], F32) for i in range(2)]
    yacc = sb(esC, "yacc", [128, 8, 1024], F32, side="right")
    of32 = [sb(esC, f"of32_{i}", [128, 512], F32) for i in range(2)]
    xf2 = [sb(esC, f"xf2_{i}", [128, 512], F32) for i in range(2)]
    r_w1p = [Res() for _ in range(NW1)]
    r_w2p = [Res() for _ in range(NW2)]
    r_hgb = [Res() for _ in range(16)]
    r_hf32 = [Res(), Res()]
    r_gateB = [Res(), Res()]
    r_yacc16 = {(oc, t2): Res() for oc in range(8) for t2 in range(2)}

    for c in range(16):
        bank = 6 + (c % 2)
        for kc in range(8):
            mm(x2n[:, kc, c * 128:(c + 1) * 128], rwt_sb[:, kc, :], bank, slice(0, 8),
               kc == 0, kc == 7, rres=[r_x2n, r_w["rwt"]])
        _ew("scalar", lambda s, c=c, bank=bank: s.copy(out=gll[:, c, :], in_=pb[bank][:, 0:8]),
            w=[r_gch], r=[rpb[bank]])

    def vdrain():
        _ew("vector", lambda v: v.drain())

    B8 = [128, 16, 8]
    _ew("vector", lambda v: v.reduce_max(gm1[:], gll[:], axis=AX.X), w=[r_gch], r=[r_gch])
    vdrain()
    _ew("vector", lambda v: v.tensor_tensor(geq[:], gll[:], gm1[:].broadcast_to(B8),
                                            ALU.is_ge), w=[r_gch], r=[r_gch])
    vdrain()
    _ew("vector", lambda v: v.scalar_tensor_tensor(gl3[:], geq[:], -1e30, gll[:],
                                                   ALU.mult, ALU.add), w=[r_gch], r=[r_gch])
    vdrain()
    _ew("vector", lambda v: v.tensor_tensor(gtl[:], gll[:], gm1[:].broadcast_to(B8),
                                            ALU.subtract), w=[r_gch], r=[r_gch])
    vdrain()
    _ew("vector", lambda v: v.reduce_max(gm2[:], gl3[:], axis=AX.X), w=[r_gch], r=[r_gch])
    vdrain()
    _ew("scalar", lambda s: s.activation(out=gp[:], in_=gtl[:], func=ACT.Exp, bias=0.0,
                                         scale=1.0), w=[r_gch], r=[r_gch])
    _ew("vector", lambda v: v.tensor_tensor(gmask[:], gll[:], gm2[:].broadcast_to(B8),
                                            ALU.is_ge), w=[r_gch], r=[r_gch])
    _ew("vector", lambda v: v.tensor_mul(gpm[:], gp[:], gmask[:]), w=[r_gch], r=[r_gch])
    vdrain()
    _ew("vector", lambda v: v.reduce_sum(gZ[:], gpm[:], axis=AX.X), w=[r_gch], r=[r_gch])
    vdrain()
    _ew("vector", lambda v: v.reciprocal(grZ[:], gZ[:]), w=[r_gch], r=[r_gch])
    vdrain()
    _ew("vector", lambda v: v.tensor_tensor(gatb[:], gpm[:], grZ[:].broadcast_to(B8),
                                            ALU.mult), w=[r_gch], r=[r_gch])
    vdrain()

    dbg("gatb", [128, 16, 8], BF16, gatb[:], [r_gch])

    firstg = [True]
    firstv = [True]
    firsts = [True]

    def bar_once(flag):
        if flag[0]:
            flag[0] = False
            return barB
        return []

    sched = [(0, e, e, True) for e in range(E)] + \
            [(1, e, (15 - e if e != 7 else 7), e != 7) for e in [7, 6, 5, 4, 3, 2, 1, 0]]

    def load_w1g(e, gi):
        for kc in range(8):
            slot = (gi * 8 + kc) % NW1
            g_dma(w1p[slot][:], w1t_d[e, kc * 128:(kc + 1) * 128, :], r_w1p[slot],
                  deps=bar_once(firstg))

    def load_w2g(e, gi):
        for hc in range(16):
            slot = (gi * 16 + hc) % NW2
            g_dma(w2p[slot][:], w2t_d[e, hc * 128:(hc + 1) * 128, :], r_w2p[slot],
                  deps=bar_once(firstg))

    load_w1g(0, 0)
    load_w2g(0, 0)
    for si, (half, e, gi, _ld) in enumerate(sched):
        for t2 in range(2):
            t = half * 2 + t2
            gsl = si % 2
            bank_g = 6 + gsl
            for c4 in range(4):
                mm(gatb[:, t * 4 + c4, e:e + 1], eyeb[:], bank_g,
                   (slice(0, 1), slice(c4 * 128, (c4 + 1) * 128)), True, True,
                   rres=[r_gch, r_w["consts"]])
            _ew("vector", lambda v, bank_g=bank_g: v.tensor_copy(
                g1row[:], pb[bank_g][0:1, 0:512]), w=[r_g1], r=[rpb[bank_g]],
                deps=bar_once(firstv))
            mm(ones_f[:], g1row[:], bank_g, slice(0, 512), True, True,
               rres=[r_g1, r_w["consts"]])
            _ew("vector", lambda v, gsl=gsl, bank_g=bank_g: v.tensor_copy(
                gateB[gsl][:], pb[bank_g][:, 0:512]), w=[r_gateB[gsl]], r=[rpb[bank_g]])
            for hc in range(16):
                bank = 2 + (hc % 2)
                for kc in range(8):
                    slot = (gi * 8 + kc) % NW1
                    mm(w1p[slot][:, hc * 128:(hc + 1) * 128],
                       x2n[:, kc, t * TT:(t + 1) * TT],
                       bank, slice(0, 512), kc == 0, kc == 7, rres=[r_w1p[slot], r_x2n])
                hsl = hc % 2
                _ew("scalar", lambda s, hsl=hsl, bank=bank: s.activation(
                    out=hf32[hsl][:], in_=pb[bank][:, 0:512], func=ACT.Gelu,
                    bias=0.0, scale=1.0), w=[r_hf32[hsl]], r=[rpb[bank]],
                    deps=bar_once(firsts))
                _ew("vector", lambda v, hc=hc, hsl=hsl, gsl=gsl: v.tensor_tensor(
                    hgb[:, hc, :], hf32[hsl][:], gateB[gsl][:], ALU.mult),
                    w=[r_hgb[hc]], r=[r_hf32[hsl], r_gateB[gsl]])
            for oc in range(8):
                bank = 4 + (oc % 2)
                for hc in range(16):
                    slot = (gi * 16 + hc) % NW2
                    mm(w2p[slot][:, oc * 128:(oc + 1) * 128], hgb[:, hc, :],
                       bank, slice(0, 512), hc == 0, hc == 15,
                       rres=[r_w2p[slot], r_hgb[hc]])
                if si % 8 == 0:
                    _ew("vector", lambda v, oc=oc, t2=t2, bank=bank: v.tensor_copy(
                        yacc[:, oc, t2 * TT:(t2 + 1) * TT], pb[bank][:, 0:512]),
                        w=[r_yacc16[(oc, t2)]], r=[rpb[bank]])
                else:
                    _ew("vector", lambda v, oc=oc, t2=t2, bank=bank: v.tensor_tensor(
                        yacc[:, oc, t2 * TT:(t2 + 1) * TT], pb[bank][:, 0:512],
                        yacc[:, oc, t2 * TT:(t2 + 1) * TT], ALU.add),
                        w=[r_yacc16[(oc, t2)]], r=[rpb[bank]])
        if si + 1 < len(sched):
            nh, ne, ngi, nld = sched[si + 1]
            if nld:
                load_w1g(ne, ngi)
                load_w2g(ne, ngi)
        if si % 8 != 7:
            continue
        for t2 in range(2):
            t = half * 2 + t2
            for oc in range(8):
                ro = r_out[(t, oc)]
                _ew("gpsimd", lambda g, oc=oc, t2=t2, t=t: g.dma_start(
                    out=outT[oc * 128:(oc + 1) * 128, t * TT:(t + 1) * TT],
                    in_=yacc[:, oc, t2 * TT:(t2 + 1) * TT], accum_op=ALU.add),
                    w=[ro], r=[r_yacc16[(oc, t2)]], dma=True)

    fin = P.cnt["sync"]
    fin_g = P.cnt["gpsimd"]

    with nc.Block() as block:
        sems = sems_h

        @block.tensor
        def _(t):
            for th in P.q["tensor"]:
                th(t, sems)

        @block.vector
        def _(v):
            for th in P.q["vector"]:
                th(v, sems)

        @block.scalar
        def _(s):
            for th in P.q["scalar"]:
                th(s, sems)

        @block.gpsimd
        def _(g):
            for th in P.q["gpsimd"]:
                th(g, sems)

        @block.sync
        def _(s):
            for th in P.q["sync"]:
                th(s, sems)
            s.wait_ge(sems["sync"], fin)
            s.wait_ge(sems["gpsimd"], fin_g)

    return nc


def _consts(s_idx):
    gam = 1.0 - np.exp2(-5.0 - np.arange(NH, dtype=np.float64))
    idx = np.arange(CH)
    cdt = np.zeros((128, 512), np.float64)
    for h in range(NH):
        diff = idx[None, :] - idx[:, None]
        m = np.where(diff >= 0, gam[h] ** np.maximum(diff, 0), 0.0)
        cdt[0:64, h * 64:(h + 1) * 64] = m
        cdt[64:128, h * 64:(h + 1) * 64] = m
    cdq = np.zeros((128, 512), np.float64)
    for h in range(NH):
        cdq[:, h * 64:(h + 1) * 64] = (gam[h] ** (idx + 1))[None, :]
    cdq8 = np.zeros((128, 8, 512), np.float64)
    for n in range(8):
        for h in range(NH):
            cdq8[:, n, h * 64:(h + 1) * 64] = (gam[h] ** (idx + 1) * gam[h] ** (CH * n))[None, :]
    cdk = np.zeros((128, 8), np.float64)
    pos = np.arange(128) % 64
    for h in range(NH):
        cdk[:, h] = gam[h] ** (CH - 1 - pos)
    ccoef = np.zeros((128, 32), np.float64)
    for j in range(4):
        for h in range(NH):
            if j < s_idx:
                ccoef[:, j * 8 + h] = gam[h] ** (CH * 32 * (s_idx - 1 - j))
    return (cdt.astype(np.float32), cdq.astype(np.float32),
            cdk.astype(np.float32), ccoef.astype(np.float32), cdq8.astype(np.float32))


def kernel(L0_attn_hidden_states, L0_attn_rel_pos, L0_attn_Wq, L0_attn_Wk,
           L0_attn_Wv, L0_attn_Wo, L0_ffn_router_w, L0_ffn_w1, L0_ffn_w2):
    x = np.ascontiguousarray(np.asarray(L0_attn_hidden_states, dtype=np.float32))
    wq = np.ascontiguousarray(np.asarray(L0_attn_Wq, dtype=np.float32))
    wk = np.ascontiguousarray(np.asarray(L0_attn_Wk, dtype=np.float32))
    wv = np.ascontiguousarray(np.asarray(L0_attn_Wv, dtype=np.float32))
    wo = np.ascontiguousarray(np.asarray(L0_attn_Wo, dtype=np.float32))
    rw = np.asarray(L0_ffn_router_w, dtype=np.float32)
    w1 = np.asarray(L0_ffn_w1, dtype=np.float32)
    w2 = np.asarray(L0_ffn_w2, dtype=np.float32)

    xT = np.ascontiguousarray(np.transpose(x, (0, 2, 1)))
    w1t = np.ascontiguousarray(np.transpose(w1, (0, 2, 1)))
    w2t = np.ascontiguousarray(np.transpose(w2, (0, 2, 1)))
    rwt = np.ascontiguousarray(rw.T)
    eye = np.eye(128, dtype=np.float32)

    nc = build()
    in_maps = []
    for c in range(8):
        b, s = c // 4, c % 4
        cdt, cdq, cdk, ccoef, cdq8 = _consts(s)
        in_maps.append({
            "xT": np.ascontiguousarray(xT[b, :, s * SEG:(s + 1) * SEG]),
            "wq": wq, "wk": wk, "wv": wv, "wo": wo,
            "w1t": w1t, "w2t": w2t, "rwt": rwt, "eye": eye,
            "cdt": cdt, "cdq": cdq, "cdk": cdk, "ccoef": ccoef, "cdq8": cdq8,
        })
    res = run_bass_kernel_spmd(nc, in_maps, core_ids=list(range(8)))
    out = np.empty((2, 8192, 1024), np.float32)
    for c in range(8):
        b, s = c // 4, c % 4
        out[b, s * SEG:(s + 1) * SEG, :] = res.results[c]["outT"].T
    return out
